# version 08
# baseline (speedup 1.0000x reference)
import contextlib
import numpy as np
import ml_dtypes
import concourse.bass as bass
import concourse.mybir as mybir
from concourse.bass_utils import run_bass_kernel_spmd

F32 = mybir.dt.float32
BF16 = mybir.dt.bfloat16
AF = mybir.ActivationFunctionType
ALU = mybir.AluOpType
AX = mybir.AxisListType
BF = ml_dtypes.bfloat16

B, N, D = 16, 2048, 1024
H, DP, DH, WIN = 16, 16, 64, 64
NCORES = 8
TOK = (B // NCORES) * N
MACRO = 256
NM_FULL = TOK // MACRO
MPS = N // MACRO
DFF = 4 * D
NK = D // 128
NFC = DFF // 128
EPS_LN = 1e-5
EPS_Q = 1e-5
EPS_DEN = 1e-8
NW = float(DFF * D)
SRH = 0.7071067811865476

ENGINES = ("tensor", "vector", "scalar", "gpsimd", "sync")
_LAST_SCHED = None
_STOP_PHASE = None
_DUMP = None


class _StopBuild(Exception):
    pass


def _ckpt(name):
    if _STOP_PHASE is not None and name == _STOP_PHASE:
        raise _StopBuild()


class Ev:
    __slots__ = ("eng", "kind", "needed", "val", "idx")

    def __init__(self, eng, kind, idx=0):
        self.eng, self.kind, self.needed, self.val = eng, kind, False, None
        self.idx = idx


def flat(*evs):
    out = []
    for e in evs:
        if e is None:
            continue
        if isinstance(e, (list, tuple)):
            out.extend(x for x in e if x is not None)
        else:
            out.append(e)
    return out


class Sched:

    def __init__(self):
        self.ops = {e: [] for e in ENGINES}

    def __init__Last(self):
        pass

    def op(self, eng, fn, after=(), kind="c", hard=()):
        if not hasattr(self, "last_ev"):
            self.last_ev = {}
        ev = Ev(eng, kind, len(self.ops[eng]))
        best = {}
        for a in flat(*after):
            if a.eng == eng and a.kind == kind and (
                    eng in ("tensor", "sync") or kind == "d"):
                continue
            key = (a.eng, a.kind)
            if key not in best or a.idx > best[key].idx:
                best[key] = a
        for a in flat(*hard):
            key = (a.eng, a.kind)
            if key not in best or a.idx > best[key].idx:
                best[key] = a
        deps = list(best.values())
        for a in deps:
            a.needed = True
        self.ops[eng].append((ev, fn, deps))
        self.last_ev[eng] = ev
        return ev

    def resolve(self):
        for eng in ENGINES:
            cnt = {"c": 0, "d": 0}
            for ev, _f, _d in self.ops[eng]:
                if ev.needed or ev.kind == "d":
                    cnt[ev.kind] += 16 if ev.kind == "d" else 1
                    ev.val = cnt[ev.kind]

    def replay(self, eng_name, eng, sems):
        waited = {}
        for ev, fn, deps in self.ops[eng_name]:
            for d in deps:
                key = (d.eng, d.kind)
                if waited.get(key, 0) < d.val:
                    eng.wait_ge(sems[key], d.val)
                    waited[key] = d.val
            inst = fn(eng)
            if ev.needed or ev.kind == "d":
                inst.then_inc(sems[(ev.eng, ev.kind)], 16 if ev.kind == "d" else 1)


def build_nc(nm=NM_FULL, detect_races=False, barrier=False, emit=None):
    nc = bass.Bass("TRN2", target_bir_lowering=False,
                   detect_race_conditions=detect_races,
                   dynamic_dma_scratch_size=16384)
    tok = nm * MACRO

    x_in = nc.dram_tensor("x_in", [tok, D], F32, kind="ExternalInput")
    x_out = nc.dram_tensor("x_out", [tok, D], F32, kind="ExternalOutput")
    wqt_d = nc.dram_tensor("wqt", [D, 512], F32, kind="ExternalInput")
    wkt_d = nc.dram_tensor("wkt", [D, 512], F32, kind="ExternalInput")
    wvt_d = nc.dram_tensor("wvt", [D, D], F32, kind="ExternalInput")
    w1t_d = nc.dram_tensor("w1t", [D, DFF], F32, kind="ExternalInput")
    w2t_d = nc.dram_tensor("w2t", [DFF, D], F32, kind="ExternalInput")
    bq_d = nc.dram_tensor("bq_r", [1, 512], F32, kind="ExternalInput")
    bk_d = nc.dram_tensor("bk_r", [1, 512], F32, kind="ExternalInput")
    bv_d = nc.dram_tensor("bv_r", [1, D], F32, kind="ExternalInput")
    b1c_d = nc.dram_tensor("b1c", [128, NFC], F32, kind="ExternalInput")
    b2_d = nc.dram_tensor("b2_r", [1, D], F32, kind="ExternalInput")
    identb_d = nc.dram_tensor("identb", [128, 128], BF16, kind="ExternalInput")
    tri_d = nc.dram_tensor("tri", [128, 128], BF16, kind="ExternalInput")
    htri_d = nc.dram_tensor("htri", [128, 128], BF16, kind="ExternalInput")
    hfull_d = nc.dram_tensor("hfull", [128, 128], BF16, kind="ExternalInput")
    bmask_d = nc.dram_tensor("bmask", [128, 512], BF16, kind="ExternalInput")
    onesc_d = nc.dram_tensor("ones_c", [128, 1], BF16, kind="ExternalInput")
    onesr_d = nc.dram_tensor("ones_r", [1, 256], BF16, kind="ExternalInput")
    e64_d = nc.dram_tensor("e64_r", [1, 128], BF16, kind="ExternalInput")
    epsr_d = nc.dram_tensor("eps_r", [1, 256], BF16, kind="ExternalInput")
    w2q_spill = nc.dram_tensor("w2q_spill", [DFF, D], BF16, kind="Internal")

    S = Sched()
    global _LAST_SCHED
    _LAST_SCHED = S
    st = contextlib.ExitStack()
    sb = lambda name, shape, dt: st.enter_context(nc.sbuf_tensor(name, shape, dt))
    ps = lambda name, shape, dt: st.enter_context(nc.psum_tensor(name, shape, dt))

    s_wqt = sb("s_wqt", [128, NK * 512], BF16)
    s_wkt = sb("s_wkt", [128, NK * 512], BF16)
    s_wvt = sb("s_wvt", [128, NK * D], BF16)
    s_w1qt = sb("s_w1qt", [128, NK * DFF], BF16)
    s_w2s = sb("s_w2s", [128, 3 * D], BF16)
    s_bq = sb("s_bq", [1, 512], BF16)
    s_bk = sb("s_bk", [1, 512], BF16)
    s_bv = sb("s_bv", [1, D], BF16)
    s_b2 = sb("s_b2", [1, D], BF16)
    s_b1c = sb("s_b1c", [128, NFC], F32)
    s_idb = sb("s_idb", [128, 128], BF16)
    s_tri = sb("s_tri", [128, 128], BF16)
    s_htri = sb("s_htri", [128, 128], BF16)
    s_hfull = sb("s_hfull", [128, 128], BF16)
    s_bmask = sb("s_bmask", [128, 512], BF16)
    s_onesc = sb("s_onesc", [128, 1], BF16)
    s_onesr = sb("s_onesr", [1, 256], BF16)
    s_e64 = sb("s_e64", [1, 128], BF16)
    s_epsr = sb("s_epsr", [1, 256], BF16)
    s_onesf = sb("s_onesf", [128, 1], F32)
    s_onesfr = sb("s_onesfr", [1, 128], F32)

    s_So = sb("s_So", [128, H * 2 * 66], F32)
    s_Sob = sb("s_Sob", [128, H * 2 * 66], BF16)
    s_Sq = sb("s_Sq", [128, 4 * 66], F32)
    s_Sqb = sb("s_Sqb", [128, 4 * 66], BF16)
    s_S1 = sb("s_S1", [1, H * 66], F32)
    s_S1b = sb("s_S1b", [1, H * 66], BF16)

    s_x = sb("s_x", [128, 2 * 2048], F32)
    s_h = sb("s_h", [128, 2048], BF16)
    s_scr = s_h[:].bitcast(F32)
    s_ht = sb("s_ht", [128, NK * 256], BF16)
    s_qt = sb("s_qt", [128, 4 * 256], BF16)
    s_kt = sb("s_kt", [128, 4 * 256], BF16)
    s_ktp = sb("s_ktp", [128, 4 * 128], BF16)
    s_qtok = sb("s_qtok", [128, 2 * 512], BF16)
    s_ktok = sb("s_ktok", [128, 2 * 512], BF16)
    s_vaug = sb("s_vaug", [128, 2 * H * 128], BF16)
    s_vaugp = sb("s_vaugp", [128, H * 128], BF16)
    s_q2tok = sb("s_q2tok", [128, 2 * 512], BF16)
    s_q2t = sb("s_q2t", [128, 2 * 512], BF16)
    s_k2 = sb("s_k2", [128, 2 * 512], BF16)
    s_aexp = sb("s_aexp", [128, 2 * 896], BF16)
    s_ao = sb("s_ao", [128, 1024], F32)
    s_dent = sb("s_dent", [128, 32], F32)
    s_h1t = sb("s_h1t", [128, NFC * 256], BF16)
    s_h1t_f = s_h1t[:].bitcast(F32)
    s_numv = s_h1t
    s_stats = sb("s_stats", [128, 8 * 16], F32)
    s_mr = sb("s_mr", [128, 40], F32)
    s_cc = sb("s_cc", [128, 4], F32)
    s_gam = sb("s_gam", [128, 4], F32)
    s_g1 = sb("s_g1", [1, 4], F32)

    PA = [ps(f"PA{i}", [128, 512], F32) for i in range(4)]
    PW = ps("PW", [128, 1024], F32)
    PT = [ps(f"PT{i}", [128, 1024], BF16) for i in range(2)]

    sems = {}
    for e in ENGINES:
        sems[(e, "c")] = st.enter_context(nc.semaphore(f"sem_{e}"))
    for e in ("sync", "scalar", "gpsimd"):
        sems[(e, "d")] = st.enter_context(nc.semaphore(f"dma_{e}"))

    T = lambda fn, after=(): S.op("tensor", fn, after)
    V = lambda fn, after=(), hard=(): S.op("vector", fn, after, hard=hard)
    Sc = lambda fn, after=(): S.op("scalar", fn, after)
    G = lambda fn, after=(): S.op("gpsimd", fn, after)
    dma_sync = lambda dst, src, after=(): S.op(
        "sync", lambda e, dst=dst, src=src: e.dma_start(dst, src), after, kind="d")
    dma_act = lambda dst, src, after=(): S.op(
        "scalar", lambda e, dst=dst, src=src: e.dma_start(dst, src), after, kind="d")
    dma_g = lambda dst, src, after=(), hard=(): S.op(
        "gpsimd", lambda e, dst=dst, src=src: e.dma_start(dst, src), after, kind="d",
        hard=hard)

    def vcopy(dst, src, after=()):
        return V(lambda e: e.tensor_copy(dst, src), after)

    def scopy(dst, src, after=()):
        return Sc(lambda e: e.copy(dst, src), after)

    _gld = []

    def dma_g_t(dst, src):
        ev = dma_g(dst, src, hard=[_gld[-4]] if len(_gld) >= 4 else ())
        _gld.append(ev)
        return ev

    for k in range(NK):
        dma_g_t(s_wqt[:, 512 * k:512 * (k + 1)], wqt_d[128 * k:128 * (k + 1), :])
        dma_g_t(s_wkt[:, 512 * k:512 * (k + 1)], wkt_d[128 * k:128 * (k + 1), :])
        dma_g_t(s_wvt[:, D * k:D * (k + 1)], wvt_d[128 * k:128 * (k + 1), :])
    for dst, src in ((s_bq, bq_d), (s_bk, bk_d), (s_bv, bv_d), (s_b2, b2_d),
                     (s_b1c, b1c_d), (s_idb, identb_d),
                     (s_tri, tri_d), (s_htri, htri_d), (s_hfull, hfull_d),
                     (s_bmask, bmask_d), (s_onesc, onesc_d), (s_onesr, onesr_d),
                     (s_e64, e64_d), (s_epsr, epsr_d)):
        ev_const_ld = dma_g_t(dst[:], src[:])
    ev_loads = ev_const_ld

    V(lambda e: e.memset(s_cc[:, 0:1], SRH))
    V(lambda e: e.memset(s_cc[:, 1:2], -0.5))
    V(lambda e: e.memset(s_cc[:, 2:3], EPS_LN))
    V(lambda e: e.memset(s_cc[:, 3:4], EPS_DEN))
    e_onesf = V(lambda e: e.memset(s_onesf[:], 1.0))
    e_onesfr = V(lambda e: e.memset(s_onesfr[:], 1.0))
    ev_vo = [V(lambda e: e.memset(
        s_vaug[:].rearrange("p (s h c) -> p s h c", s=2, h=H)[:, :, :, 64:65], 1.0)),
        V(lambda e: e.memset(
            s_vaug[:].rearrange("p (s h c) -> p s h c", s=2, h=H)[:, :, :, 65:128],
            0.0))]
    ev_vop = [V(lambda e: e.memset(
        s_vaugp[:].rearrange("p (h c) -> p h c", h=H)[:, :, 64:65], 1.0)),
        V(lambda e: e.memset(
            s_vaugp[:].rearrange("p (h c) -> p h c", h=H)[:, :, 65:128], 0.0))]

    scr_w = s_x[:, 2048:4096]
    scr_y = s_h1t_f[:, 0:2048]
    scr_g = s_h1t_f[:, 2048:4096]

    e_a1 = V(lambda e: e.memset(s_gam[:, 0:1], 0.0))
    e_a2 = V(lambda e: e.memset(s_gam[:, 2:3], 0.0))
    prev = None
    for c in range(16):
        rk, hf = c // 2, c % 2
        ldw = dma_sync(scr_w, w1t_d[128 * rk:128 * (rk + 1), 2048 * hf:2048 * (hf + 1)],
                       after=[prev])
        red = V(lambda e: e.reduce_sum(s_gam[:, 1:2], scr_w, axis=AX.X,
                                       apply_absolute_value=True), after=[ldw])
        prev = V(lambda e: e.tensor_add(s_gam[:, 0:1], s_gam[:, 0:1], s_gam[:, 1:2]),
                 after=[red, e_a1])
    for c in range(32):
        ldw = dma_sync(scr_w[:, 0:1024], w2t_d[128 * c:128 * (c + 1), :],
                       after=[prev])
        red = V(lambda e: e.reduce_sum(s_gam[:, 3:4], scr_w[:, 0:1024], axis=AX.X,
                                       apply_absolute_value=True), after=[ldw])
        prev = V(lambda e: e.tensor_add(s_gam[:, 2:3], s_gam[:, 2:3], s_gam[:, 3:4]),
                 after=[red, e_a2])
    e_gmm = T(lambda e: e.matmul(
        PA[0][0:1, 0:2], s_onesf[:],
        s_gam[:].rearrange("p (a u) -> p a u", a=2, u=2)[:, :, 0],
        start=True, stop=True), after=[prev, e_onesf])
    e_tot = scopy(s_g1[0:1, 0:2], PA[0][0:1, 0:2], after=[e_gmm])
    e_s = V(lambda e: e.tensor_single_scalar(s_g1[0:1, 2:4], s_g1[0:1, 0:2],
                                             NW * EPS_Q, ALU.add), after=[e_tot])
    e_s = V(lambda e: e.reciprocal(s_g1[0:1, 2:4], s_g1[0:1, 2:4]), after=[e_s])
    e_s = V(lambda e: e.tensor_single_scalar(s_g1[0:1, 2:4], s_g1[0:1, 2:4],
                                             NW, ALU.mult), after=[e_s])
    e_bcm = T(lambda e: e.matmul(PA[0][:, 4:6], s_onesfr[:], s_g1[0:1, 2:4],
                                 start=True, stop=True), after=[e_s, e_onesfr, e_tot])
    e_bc = scopy(s_gam[:, 0:2], PA[0][:, 4:6], after=[e_bcm])

    def quant_chunk(dram_slice, out_dst, ncols, scol, after):
        ldw = dma_sync(scr_w[:, 0:ncols], dram_slice, after=after)
        ey = V(lambda e: e.tensor_scalar(scr_y[:, 0:ncols], scr_w[:, 0:ncols],
                                         s_gam[:, scol:scol + 1], None, ALU.mult),
               after=[ldw, e_bc])
        eg1 = G(lambda e: e.tensor_single_scalar(scr_g[:, 0:ncols], scr_y[:, 0:ncols],
                                                 0.5, ALU.is_gt), after=[ey])
        eg2 = V(lambda e: e.tensor_single_scalar(scr_w[:, 0:ncols], scr_y[:, 0:ncols],
                                                 -0.5, ALU.is_lt), after=[ey])
        eq = V(lambda e: e.tensor_sub(out_dst, scr_g[:, 0:ncols], scr_w[:, 0:ncols]),
               after=[eg1, eg2])
        return eq

    prevq = e_bc
    for c in range(16):
        rk, hf = c // 2, c % 2
        dst = s_w1qt[:, DFF * rk + 2048 * hf: DFF * rk + 2048 * (hf + 1)]
        prevq = quant_chunk(w1t_d[128 * rk:128 * (rk + 1), 2048 * hf:2048 * (hf + 1)],
                            dst, 2048, 0, [prevq])
    ev_spill = None
    for c in range(32):
        sl = c % 2
        dst = s_w2s[:, 2048 * sl:2048 * sl + 1024]
        eq = quant_chunk(w2t_d[128 * c:128 * (c + 1), :], dst, 1024, 1,
                         [prevq, ev_spill])
        ev_spill = dma_g(w2q_spill[128 * c:128 * (c + 1), :], dst, after=[eq])
        prevq = eq
    ev_spill_done = ev_spill
    ev_quant_done = prevq

    ev_fold = [V(lambda e: e.memset(s_So[:], 0.0)),
               V(lambda e: e.memset(s_Sq[:], 0.0)),
               V(lambda e: e.memset(s_S1[:], 0.0))]
    ev_cast = [V(lambda e: e.memset(s_Sob[:], 0.0)),
               V(lambda e: e.memset(s_Sqb[:], 0.0)),
               V(lambda e: e.memset(s_S1b[:], 0.0))]

    x_t = x_in.rearrange("(m s p) d -> m s p d", s=2, p=128)
    xo_t = x_out.rearrange("(m s p) d -> m s p d", s=2, p=128)
    ev_xload, ev_outdma = {}, {}

    def load_x(m, extra=()):
        if m >= nm:
            return
        slot = (m % 2) * 2048
        after = flat(ev_outdma.get(m - 2), *extra)
        ev_xload[m] = [
            dma_sync(s_x[:, slot:slot + 1024], x_t[m, 0], after=after),
            dma_sync(s_x[:, slot + 1024:slot + 2048], x_t[m, 1], after=after)]

    load_x(0)
    load_x(1, extra=[[ev_quant_done]])

    w2_ev, w2_used = {}, {}

    def w2_fetch(gci):
        if gci >= nm * NFC:
            return
        sl = gci % 3
        fc = gci % NFC
        after = flat(ev_spill_done, w2_used.get(gci - 3))
        w2_ev[gci] = dma_act(s_w2s[:, D * sl:D * (sl + 1)],
                             w2q_spill[128 * fc:128 * (fc + 1), :], after=after)

    for i in range(3):
        w2_fetch(i)

    def layernorm(src, dst, inst_id, after):
        c0 = 16 * inst_id
        raw = s_stats[:, c0:c0 + 12]
        mcols = raw.rearrange("p (g c) -> p g c", g=4)[:, :, 1]
        m2cols = raw.rearrange("p (g c) -> p g c", g=4)[:, :, 2]
        tmp4 = s_stats[:, c0 + 12:c0 + 16]
        mcol = s_mr[:, 5 * inst_id + 0:5 * inst_id + 1]
        vcol = s_mr[:, 5 * inst_id + 1:5 * inst_id + 2]
        rcol = s_mr[:, 5 * inst_id + 2:5 * inst_id + 3]
        ncol = s_mr[:, 5 * inst_id + 3:5 * inst_id + 4]
        tcol = s_mr[:, 5 * inst_id + 4:5 * inst_id + 5]
        e1 = V(lambda e: e.bn_stats(s_stats[:, c0:c0 + 6], src[:, 0:512]),
               after=after)
        e2 = V(lambda e: e.bn_stats(s_stats[:, c0 + 6:c0 + 12], src[:, 512:1024]))
        em = V(lambda e: e.tensor_reduce(mcol, mcols, AX.X, ALU.add),
               after=[e1, e2])
        emq = V(lambda e: e.tensor_tensor(tmp4, mcols, mcols, ALU.mult),
                after=[e1, e2])
        e2q = V(lambda e: e.tensor_reduce(tcol, tmp4, AX.X, ALU.add),
                after=[emq])
        ev2 = V(lambda e: e.tensor_reduce(vcol, m2cols, AX.X, ALU.add),
                after=[e1, e2])
        e5a = V(lambda e: e.tensor_single_scalar(mcol, mcol, 0.25, ALU.mult),
                after=[em])
        e5b = V(lambda e: e.tensor_tensor(ncol, mcol, mcol, ALU.mult),
                after=[e5a])
        e5c = V(lambda e: e.scalar_tensor_tensor(tcol, tcol, 0.25, ncol,
                                                 ALU.mult, ALU.subtract),
                after=[e2q, e5b])
        e5d = V(lambda e: e.scalar_tensor_tensor(vcol, vcol, 1.0 / D, tcol,
                                                 ALU.mult, ALU.add),
                after=[ev2, e5c])
        e4 = Sc(lambda e: e.activation(rcol, vcol, AF.Ln, bias=s_cc[:, 2:3],
                                       scale=1.0), after=[e5d])
        e5 = Sc(lambda e: e.activation(rcol, rcol, AF.Exp, bias=0.0,
                                       scale=s_cc[:, 1:2]), after=[e4])
        e6 = V(lambda e: e.scalar_tensor_tensor(ncol, mcol, -1.0, rcol,
                                                ALU.mult, ALU.mult),
                after=[e5, e5b])
        e7 = Sc(lambda e: e.activation(dst, src, AF.Identity, bias=ncol, scale=rcol),
                after=[e6])
        return e7

    ev_h_free = []
    ev_ht_free = []
    ev_qk_done = []
    ev_v_done = [ev_vo]
    ev_prev_saved = []
    ev_numv_free = []
    ev_state_rd = []
    ev_pt_free = [None, None]
    ev_pa_free = [e_tot, None, None, None]
    ev_pw_free = None
    ev_sqt_rd = {0: None, 1: None}
    ev_sk2_rd = {0: None, 1: None}
    ev_sq2tok_rd = {0: None, 1: None}

    _bufs = dict(h=s_h, ht=s_ht, qt=s_qt, kt=s_kt, qtok=s_qtok, PTDBG=s_qtok, VCOPY=s_qtok, ktok=s_ktok,
                 vaug=s_vaug, q2tok=s_q2tok, q2t=s_q2t, k2=s_k2, aexp=s_aexp,
                 numv=s_numv, ao=s_ao, x=s_x, h1t=s_h1t, So=s_So, Sob=s_Sob,
                 stats=s_stats, mr=s_mr, dent=s_dent, Sq=s_Sq, S1=s_S1,
                 w1qt=s_w1qt, w2s=s_w2s, wqt=s_wqt, wvt=s_wvt, bq=s_bq,
                 onesr=s_onesr, cc=s_cc, gam=s_gam)
    try:
      for m in range(nm if emit is None else emit):
        if barrier and m > 0:
            lasts = [S.last_ev.get(e) for e in ENGINES]
            bar = []
            for e in ENGINES:
                deps = [x for x in lasts if x is not None and x.eng != e]
                if e in ("tensor", "vector", "scalar", "gpsimd"):
                    bar.append(S.op(e, lambda en: en.nop(), after=deps))
            lasts2 = bar
            for e in ("tensor", "vector", "scalar", "gpsimd"):
                S.op(e, lambda en: en.nop(), after=[x for x in bar if x.eng != e])
        sm = m % MPS
        seq0 = (sm == 0)
        slot = (m % 2) * 2048
        xs = lambda sub, slot=slot: s_x[:, slot + 1024 * sub:slot + 1024 * (sub + 1)]

        if seq0 and m > 0:
            a = flat(ev_fold, ev_cast, ev_state_rd)
            ev_fold = [V(lambda e: e.memset(s_So[:], 0.0), after=a),
                       V(lambda e: e.memset(s_Sq[:], 0.0), after=a),
                       V(lambda e: e.memset(s_S1[:], 0.0), after=a)]
            ev_cast = [V(lambda e: e.memset(s_Sob[:], 0.0), after=a),
                       V(lambda e: e.memset(s_Sqb[:], 0.0), after=a),
                       V(lambda e: e.memset(s_S1b[:], 0.0), after=a)]
        elif m > 0:
            a = flat(ev_fold, ev_state_rd)
            ev_cast = [vcopy(s_Sob[:], s_So[:], after=a),
                       vcopy(s_Sqb[:], s_Sq[:], after=a),
                       vcopy(s_S1b[:], s_S1[:], after=a)]
        ev_state_rd = []

        ld0 = [ev_loads] if m == 0 else []
        ln1 = [layernorm(xs(sub), s_h[:, 1024 * sub:1024 * (sub + 1)], sub,
                         flat(ev_xload[m], ev_h_free, ld0)) for sub in range(2)]
        _ckpt("ln1")
        ev_ht = []
        for sub in range(2):
            pt = PT[sub]
            mms = [T(lambda e, k=k, sub=sub, pt=pt: e.transpose(
                pt[:, 128 * k:128 * (k + 1)],
                s_h[:, 1024 * sub + 128 * k:1024 * sub + 128 * (k + 1)], s_idb[:]),
                after=flat(ln1[sub], ev_pt_free[sub], ld0) if k == 0 else ())
                for k in range(NK)]
            dst = s_ht[:].rearrange("p (k t) -> p k t", k=NK)[:, :, 128 * sub:128 * (sub + 1)]
            ev = scopy(dst, pt[:], after=flat(mms, ev_ht_free))
            ev_pt_free[sub] = ev
            ev_ht.append(ev)
        ev_ht_free = []

        _ckpt("ht")
        qk_banks = [(PA[0], 0), (PA[0], 256), (PA[1], 0), (PA[1], 256),
                    (PA[2], 0), (PA[2], 256), (PA[3], 0), (PA[3], 256)]
        bank_guard = [ev_pa_free[0], ev_pa_free[0], ev_pa_free[1], ev_pa_free[1],
                      ev_pa_free[2], ev_pa_free[2], ev_pa_free[3], ev_pa_free[3]]
        qmm = {i: [] for i in range(8)}
        for wi, (sW, sB) in enumerate(((s_wqt, s_bq), (s_wkt, s_bk))):
            for g in range(4):
                bi = 4 * wi + g
                bank, colo = qk_banks[bi]
                for k in range(NK):
                    qmm[bi].append(T(lambda e, k=k, g=g, sW=sW, bank=bank, colo=colo:
                                     e.matmul(bank[:, colo:colo + 256],
                                              sW[:, 512 * k + 128 * g:512 * k + 128 * (g + 1)],
                                              s_ht[:, 256 * k:256 * (k + 1)],
                                              start=(k == 0), stop=False),
                                     after=flat(ev_ht, bank_guard[bi], ld0) if k == 0 else ()))
                qmm[bi].append(T(lambda e, g=g, sB=sB, bank=bank, colo=colo:
                                 e.matmul(bank[:, colo:colo + 256],
                                          sB[0:1, 128 * g:128 * (g + 1)],
                                          s_onesr[0:1, 0:256], start=False, stop=True)))
        ev_qt_ev = [
            scopy(s_qt[:, 0:512], PA[0][:], after=flat(qmm[0], qmm[1], ev_qk_done)),
            scopy(s_qt[:, 512:1024], PA[1][:], after=flat(qmm[2], qmm[3], ev_qk_done)),
            scopy(s_kt[:, 0:512], PA[2][:], after=flat(qmm[4], qmm[5], ev_qk_done)),
            scopy(s_kt[:, 512:1024], PA[3][:], after=flat(qmm[6], qmm[7], ev_qk_done))]
        ev_pa_free = [ev_qt_ev[0], ev_qt_ev[1], ev_qt_ev[2], ev_qt_ev[3]]
        ev_qk_done = []

        _ckpt("qkv")
        ev_v_ev = []
        for sub in range(2):
            vmms = []
            for k in range(NK):
                for hf in range(2):
                    dep = flat(ev_ht, ev_pw_free, ld0) if (k == 0 and hf == 0) else ()
                    vmms.append(T(lambda e, k=k, sub=sub, hf=hf: e.matmul(
                        PW[:, 512 * hf:512 * (hf + 1)],
                        s_ht[:, 256 * k + 128 * sub:256 * k + 128 * (sub + 1)],
                        s_wvt[:, D * k + 512 * hf:D * k + 512 * (hf + 1)],
                        start=(k == 0), stop=False), after=dep))
            for hf in range(2):
                vmms.append(T(lambda e, hf=hf: e.matmul(
                    PW[:, 512 * hf:512 * (hf + 1)], s_onesr[0:1, 0:128],
                    s_bv[0:1, 512 * hf:512 * (hf + 1)], start=False, stop=True)))
            dst = s_vaug[:, 2048 * sub:2048 * (sub + 1)].rearrange(
                "p (h c) -> p h c", h=H)[:, :, 0:64]
            e = Sc(lambda e, dst=dst: e.copy(
                dst, PW[:].rearrange("p (h c) -> p h c", h=H, c=64)),
                after=flat(vmms, ev_v_done))
            ev_v_ev.append(e)
            ev_pw_free = e
        ev_v_done = []

        _ckpt("v")
        ev_qktok = []
        for si, (s_src, s_dst) in enumerate(((s_qt, s_qtok), (s_kt, s_ktok))):
            for sub in range(2):
                pt = PT[sub]
                mms = [T(lambda e, g=g, sub=sub, s_src=s_src, pt=pt: e.transpose(
                    pt[:, 128 * g:128 * (g + 1)],
                    s_src[:, 256 * g + 128 * sub:256 * g + 128 * (sub + 1)], s_idb[:]),
                    after=flat(ev_qt_ev, ev_pt_free[sub]) if g == 0 else ())
                    for g in range(4)]
                e = scopy(s_dst[:, 512 * sub:512 * (sub + 1)], pt[:, 0:512], after=mms)
                ev_pt_free[sub] = e
                ev_qktok.append(e)

        _ckpt("qktok")
        num_banks = [(PA[2], 0), (PW, 0), (PW, 512)]
        ev_nb_free = [ev_pa_free[2], flat(ev_v_ev), flat(ev_v_ev)]
        ev_aexp_free = [None, None]
        ev_gt_free = [ev_pa_free[0], ev_pa_free[1]]
        ev_ptq_free = [ev_pt_free[0], ev_pt_free[1]]
        ev_numv_ev = {}
        use_prev = (sm != 0)

        for h in range(H):
            g, s32, rot = h // 4, 32 * (h % 4), h % 2

            q2c, k2c = [], []
            for sub in range(2):
                def fq(e, sub=sub, rot=rot, g=g, s32=s32):
                    base = s_qtok[:, 512 * sub + 128 * g + s32:
                                  512 * sub + 128 * g + s32 + 16]
                    return e.scalar_tensor_tensor(
                        s_q2tok[:, 512 * rot + 256 * sub:512 * rot + 256 * (sub + 1)]
                        .rearrange("p (i j) -> p i j", i=16),
                        base.unsqueeze(2).broadcast_to([128, 16, 16]), s_cc[:, 0:1],
                        base.unsqueeze(1).broadcast_to([128, 16, 16]),
                        ALU.mult, ALU.mult)
                q2c.append(V(fq, after=flat(ev_qktok[sub], ev_sq2tok_rd[rot])))
                def fk(e, sub=sub, rot=rot, g=g, s32=s32):
                    base = s_ktok[:, 512 * sub + 128 * g + s32:
                                  512 * sub + 128 * g + s32 + 16]
                    return e.scalar_tensor_tensor(
                        s_k2[:, 512 * rot + 256 * sub:512 * rot + 256 * (sub + 1)]
                        .rearrange("p (i j) -> p i j", i=16),
                        base.unsqueeze(2).broadcast_to([128, 16, 16]), s_cc[:, 0:1],
                        base.unsqueeze(1).broadcast_to([128, 16, 16]),
                        ALU.mult, ALU.mult)
                k2c.append(V(fk, after=flat(ev_qktok[2 + sub], ev_sk2_rd[rot])))
            _ckpt(f"h{h}_q2c")
            ptq = PT[rot]
            q2mm = [T(lambda e, sub=sub, fsl=fsl, rot=rot, ptq=ptq: e.transpose(
                ptq[:, 256 * fsl + 128 * sub:256 * fsl + 128 * (sub + 1)],
                s_q2tok[:, 512 * rot + 256 * sub + 128 * fsl:
                        512 * rot + 256 * sub + 128 * (fsl + 1)], s_idb[:]),
                after=flat(q2c[sub], ev_ptq_free[rot] if (sub == 0 and fsl == 0) else None))
                for sub in range(2) for fsl in range(2)]
            ev_sq2tok_rd[rot] = q2mm[-1]
            q2ev = [scopy(s_q2t[:, 512 * rot + 256 * fsl:512 * rot + 256 * (fsl + 1)],
                          ptq[:, 256 * fsl:256 * (fsl + 1)],
                          after=flat(q2mm, ev_sqt_rd[rot]))
                    for fsl in range(2)]
            ev_ptq_free[rot] = q2ev[-1]

            _ckpt(f"h{h}_q2t")
            gb = PA[rot]
            gt = [T(lambda e, g=g, s32=s32, gb=gb: e.matmul(
                gb[:, 0:256], s_kt[s32:s32 + 32, 256 * g:256 * g + 128],
                s_qt[s32:s32 + 32, 256 * g:256 * (g + 1)],
                start=True, stop=False, tile_position=(s32, 0)),
                after=flat(ev_qt_ev, ev_gt_free[rot]))]
            gt.append(T(lambda e, g=g, s32=s32, gb=gb, st=not use_prev: e.matmul(
                gb[:, 256:384], s_kt[s32:s32 + 32, 256 * g + 128:256 * (g + 1)],
                s_qt[s32:s32 + 32, 256 * g + 128:256 * (g + 1)],
                start=False, stop=st, tile_position=(s32, 0))))
            if use_prev:
                gt.append(T(lambda e, g=g, s32=s32, gb=gb: e.matmul(
                    gb[:, 384:512], s_ktp[s32:s32 + 32, 128 * g:128 * (g + 1)],
                    s_qt[s32:s32 + 32, 256 * g:256 * g + 128],
                    start=False, stop=True, tile_position=(s32, 0)),
                    after=flat(ev_prev_saved)))

            _ckpt(f"h{h}_gt")
            ab = 896 * rot
            adep = flat(gt, ev_aexp_free[rot])
            eA = Sc(lambda e, gb=gb, ab=ab: e.activation(
                s_aexp[:, ab:ab + 384], gb[:, 0:384], AF.Square,
                bias=s_cc[:, 0:1], scale=s_cc[:, 0:1]), after=adep)
            nexp = 512 if use_prev else 384
            eE = Sc(lambda e, gb=gb, ab=ab, nexp=nexp: e.activation(
                s_aexp[:, ab + 384:ab + 384 + nexp], gb[:, 0:nexp],
                AF.Exp, bias=0.0, scale=1.0), after=adep)

            _ckpt(f"h{h}_eAE")
            eM1 = V(lambda e, ab=ab: e.tensor_mul(
                s_aexp[:, ab:ab + 128], s_aexp[:, ab:ab + 128], s_tri[:]),
                after=flat(eA, ld0))
            eM2 = V(lambda e, ab=ab: e.tensor_mul(
                s_aexp[:, ab + 256:ab + 384], s_aexp[:, ab + 256:ab + 384], s_tri[:]),
                after=[eA])
            if _STOP_PHASE == f"h{h}_dummy":
                import os as _os
                for _i in range(int(_os.environ.get("NDUMMY", "1"))):
                    V(lambda e: e.tensor_copy(s_h[:, 0:128], s_tri[:]))
                raise _StopBuild()
            _ckpt(f"h{h}_eM12")
            eM3 = V(lambda e, ab=ab, nexp=nexp: e.tensor_mul(
                s_aexp[:, ab + 384:ab + 384 + nexp],
                s_aexp[:, ab + 384:ab + 384 + nexp], s_bmask[:, 0:nexp]), after=[eE])

            nb, ncol = num_banks[h % 3]
            va0 = s_vaug[:, 128 * h:128 * h + 128]
            va1 = s_vaug[:, 2048 + 128 * h:2048 + 128 * h + 128]
            vap = s_vaugp[:, 128 * h:128 * h + 128]
            ndep = flat(eM1, eM2, eM3, ev_v_ev, ev_nb_free[h % 3], ld0)
            nmm = []
            sflag = [True]

            def NM(fn, dep=()):
                s0 = sflag[0]
                sflag[0] = False
                nmm.append(T(lambda e, s0=s0, fn=fn: fn(e, s0), after=dep))

            NM(lambda e, s0, nb=nb, nc_=ncol, va0=va0, ab=ab: e.matmul(
                nb[:, nc_:nc_ + 256], va0, s_aexp[:, ab:ab + 256],
                start=s0, stop=False), dep=ndep)
            NM(lambda e, s0, nb=nb, nc_=ncol, va0=va0: e.matmul(
                nb[:, nc_:nc_ + 128], va0, s_htri[:], start=s0, stop=False))
            NM(lambda e, s0, nb=nb, nc_=ncol, va0=va0: e.matmul(
                nb[:, nc_ + 128:nc_ + 256], va0, s_hfull[:], start=s0, stop=False))
            NM(lambda e, s0, nb=nb, nc_=ncol, va0=va0, ab=ab: e.matmul(
                nb[:, nc_ + 256:nc_ + 384], va0, s_aexp[:, ab + 384:ab + 512],
                start=s0, stop=False))
            NM(lambda e, s0, nb=nb, nc_=ncol, va0=va0, ab=ab: e.matmul(
                nb[:, nc_ + 384:nc_ + 512], va0, s_aexp[:, ab + 512:ab + 640],
                start=s0, stop=False))
            _ckpt(f"h{h}_nm5")
            NM(lambda e, s0, nb=nb, nc_=ncol, va1=va1, ab=ab: e.matmul(
                nb[:, nc_ + 128:nc_ + 256], va1, s_aexp[:, ab + 256:ab + 384],
                start=s0, stop=False))
            NM(lambda e, s0, nb=nb, nc_=ncol, va1=va1, ab=ab: e.matmul(
                nb[:, nc_ + 384:nc_ + 512], va1, s_aexp[:, ab + 640:ab + 768],
                start=s0, stop=False))
            _ckpt(f"h{h}_nm7")
            if use_prev:
                NM(lambda e, s0, nb=nb, nc_=ncol, vap=vap, ab=ab: e.matmul(
                    nb[:, nc_ + 256:nc_ + 384], vap, s_aexp[:, ab + 768:ab + 896],
                    start=s0, stop=False), dep=flat(ev_prev_saved))
            _ckpt(f"h{h}_nm8")
            srdep = flat(ev_cast, q2ev)
            NM(lambda e, s0, nb=nb, nc_=ncol, h=h, rot=rot: e.matmul(
                nb[0:65, nc_:nc_ + 256],
                s_Sob[:, (2 * h) * 66:(2 * h) * 66 + 65],
                s_q2t[:, 512 * rot:512 * rot + 256],
                start=s0, stop=False), dep=srdep)
            NM(lambda e, s0, nb=nb, nc_=ncol, g=g, s32=s32: e.matmul(
                nb[0:65, nc_:nc_ + 256], s_Sqb[s32:s32 + 16, 66 * g:66 * g + 65],
                s_qt[s32:s32 + 16, 256 * g:256 * (g + 1)],
                start=s0, stop=False, tile_position=(s32, 0)), dep=srdep)
            NM(lambda e, s0, nb=nb, nc_=ncol, h=h, rot=rot: e.matmul(
                nb[0:65, nc_:nc_ + 256],
                s_Sob[:, (2 * h + 1) * 66:(2 * h + 1) * 66 + 65],
                s_q2t[:, 512 * rot + 256:512 * rot + 512],
                start=s0, stop=False), dep=srdep)
            NM(lambda e, s0, nb=nb, nc_=ncol, h=h: e.matmul(
                nb[0:65, nc_:nc_ + 256], s_S1b[0:1, 66 * h:66 * h + 65],
                s_onesr[0:1, 0:256], start=s0, stop=False), dep=srdep)
            NM(lambda e, s0, nb=nb, nc_=ncol: e.matmul(
                nb[0:65, nc_:nc_ + 256], s_e64[0:1, 0:65], s_epsr[0:1, 0:256],
                start=s0, stop=False))
            _ckpt(f"h{h}_eps")
            NM(lambda e, s0, nb=nb, nc_=ncol, va1=va1: e.matmul(
                nb[:, nc_ + 128:nc_ + 256], va1, s_htri[:], start=s0, stop=True))
            last_nmm = nmm[-1]
            ev_state_rd.extend(nmm[-4:])
            ev_sqt_rd[rot] = last_nmm
            ev_aexp_free[rot] = last_nmm

            _ckpt(f"h{h}_num")
            updep = flat(eA, eE, ev_v_ev, k2c, ld0)
            up = []

            def up_so(msl, ks, start, stop, dep=()):
                up.append(T(lambda e, msl=msl, ks=ks, rot=rot, h=h, gb=gb,
                            start=start, stop=stop: e.matmul(
                    gb[:, 65 * msl:65 * (msl + 1)],
                    s_k2[:, 512 * rot + 256 * ks + 128 * msl:
                         512 * rot + 256 * ks + 128 * (msl + 1)],
                    s_vaug[:, 2048 * ks + 128 * h:2048 * ks + 128 * h + 65],
                    start=start, stop=stop), after=dep))

            up_so(0, 0, True, False, dep=updep)
            up_so(0, 1, False, False)
            for ks in range(2):
                up.append(T(lambda e, ks=ks, h=h, g=g, s32=s32, gb=gb: e.matmul(
                    gb[s32:s32 + 16, 130:195],
                    s_ktok[:, 512 * ks + 128 * g + s32:512 * ks + 128 * g + s32 + 16],
                    s_vaug[:, 2048 * ks + 128 * h:2048 * ks + 128 * h + 65],
                    start=False, stop=False, tile_position=(0, s32)),
                    after=flat(ev_qktok[2], ev_qktok[3]) if ks == 0 else ()))
            for ks in range(2):
                up.append(T(lambda e, ks=ks, h=h, gb=gb: e.matmul(
                    gb[0:1, 195:260], s_onesc[:, 0:1],
                    s_vaug[:, 2048 * ks + 128 * h:2048 * ks + 128 * h + 65],
                    start=False, stop=False)))
            up_so(1, 0, False, False)
            up_so(1, 1, False, True)
            ev_sk2_rd[rot] = up[-1]

            _ckpt(f"h{h}_upd")
            folddep = flat(up, ev_fold)
            sod = s_So[:, 2 * h * 66:(2 * h + 2) * 66].rearrange(
                "p (a c) -> p a c", a=2)[:, :, 0:65]
            f1 = V(lambda e, sod=sod, gb=gb: e.tensor_add(
                sod, sod, gb[:, 0:130].rearrange("p (a c) -> p a c", a=2)),
                after=folddep)
            sqd = s_Sq[s32:s32 + 16, 66 * g:66 * g + 65]
            f2 = V(lambda e, sqd=sqd, gb=gb, s32=s32: e.tensor_add(
                sqd, sqd, gb[s32:s32 + 16, 130:195]), after=folddep)
            s1d = s_S1[0:1, 66 * h:66 * h + 65]
            f3 = V(lambda e, s1d=s1d, gb=gb: e.tensor_add(
                s1d, s1d, gb[0:1, 195:260]), after=folddep)
            ev_fold = flat(ev_fold, f1, f2, f3)
            ev_gt_free[rot] = f3

            evd = flat(last_nmm, ev_numv_free,
                       ev_quant_done if m == 0 else None)
            ev_e = scopy(s_numv[:, 512 * h:512 * (h + 1)],
                         nb[:, ncol:ncol + 512], after=evd)
            ev_numv_ev[h] = ev_e
            ev_nb_free[h % 3] = [ev_e]
            _ckpt(f"head{h}")

        attn_last_T = up[-1]
        if (sm != MPS - 1) and (m + 1 < nm):
            esk = vcopy(s_ktp[:].rearrange("p (g t) -> p g t", g=4),
                        s_kt[:].rearrange("p (g t) -> p g t", g=4)[:, :, 128:256],
                        after=flat(attn_last_T, ev_prev_saved))
            esv = vcopy(s_vaugp[:, 0:2048], s_vaug[:, 2048:4096],
                        after=flat(attn_last_T, ev_prev_saved,
                                   ev_vop if m == 0 else None))
            ev_prev_saved = [esk, esv]
        else:
            ev_prev_saved = []

        _ckpt("attn")
        pt_guard = [ev_ptq_free[0], ev_ptq_free[1]]
        ev_h_users = []
        ln2_ev = []
        ev_ao_free = []
        last_rc = None

        def round_tr(se, sub, guards):
            mms = []
            for h_ in range(H):
                bi = h_ // 8
                dep = flat(ev_numv_ev[h_], guards[bi])
                guards[bi] = None
                mms.append(T(lambda e, h_=h_, se=se, sub=sub, bi=bi: e.transpose(
                    PT[bi][:, 128 * (h_ % 8):128 * (h_ % 8 + 1)],
                    s_numv[:, 512 * h_ + 256 * se + 128 * sub:
                           512 * h_ + 256 * se + 128 * (sub + 1)], s_idb[:]),
                    after=dep))
            return mms

        def divide_to(dst, after, eps):
            do = 16 if eps else 0
            dev = [Sc(lambda e, bi=bi, do=do: e.copy(
                s_dent[:, do + 8 * bi:do + 8 * (bi + 1)],
                PT[bi][:].rearrange("p (h v) -> p h v", h=8)[:, :, 64]),
                after=after) for bi in range(2)]
            rev = [V(lambda e, bi=bi, do=do: e.reciprocal(
                s_dent[:, do + 8 * bi:do + 8 * (bi + 1)],
                s_dent[:, do + 8 * bi:do + 8 * (bi + 1)]), after=[dev[bi]])
                for bi in range(2)]
            evs = []
            for bi in range(2):
                evs.append(V(lambda e, bi=bi, dst=dst, do=do: e.tensor_tensor(
                    dst[:, 512 * bi:512 * (bi + 1)].rearrange(
                        "p (h v) -> p h v", h=8),
                    PT[bi][:].rearrange("p (h v) -> p h v", h=8)[:, :, 0:64],
                    s_dent[:, do + 8 * bi:do + 8 * (bi + 1)].unsqueeze(2)
                    .broadcast_to([128, 8, 64]), ALU.mult), after=[rev[bi]]))
            return evs

        for sub in range(2):
            rl = round_tr(0, sub, pt_guard)
            _ckpt(f"bk{sub}_rl")
            e_lin = divide_to(s_scr[:, 0:1024],
                              flat(rl, ev_h_free if (m == 0 and sub == 0) else None,
                                   ev_h_users), eps=True)
            _ckpt(f"bk{sub}_lin")
            rc = round_tr(1, sub, [e_lin, e_lin])
            _ckpt(f"bk{sub}_rc")
            e_loc = divide_to(s_ao[:], flat(rc, ev_ao_free), eps=False)
            _ckpt(f"bk{sub}_loc")
            e_add = V(lambda e: e.tensor_add(s_ao[:], s_ao[:], s_scr[:, 0:1024]),
                      after=flat(e_loc, e_lin))
            pt_guard = [e_loc[0], e_loc[1]]
            last_rc = rc[-1]
            _ckpt(f"bk{sub}_add")
            e_ln2 = layernorm(s_ao[:], s_scr[:, 0:1024], 2 + sub, [e_add])
            _ckpt(f"bk{sub}_ln2")
            xa = xs(sub)
            e_res = V(lambda e, xa=xa: e.tensor_add(
                xa, xa, s_scr[:, 0:1024]), after=[e_ln2])
            ev_h_users.append(e_res)
            ev_ao_free = [e_ln2]
            ln2_ev.append(e_res)
            _ckpt(f"bk{sub}_res")
        ev_pt_free = [pt_guard[0], pt_guard[1]]
        ev_numv_free = [last_rc]

        ev_qk_done = flat(attn_last_T, ev_prev_saved)
        ev_v_done = flat(attn_last_T, ev_prev_saved)

        _ckpt("backend")
        ln3 = [layernorm(xs(sub), s_h[:, 1024 * sub:1024 * (sub + 1)], 4 + sub,
                         flat(ln2_ev[sub], ev_h_users)) for sub in range(2)]
        ev_h2t = []
        for sub in range(2):
            pt = PT[sub]
            mms = [T(lambda e, k=k, sub=sub, pt=pt: e.transpose(
                pt[:, 128 * k:128 * (k + 1)],
                s_h[:, 1024 * sub + 128 * k:1024 * sub + 128 * (k + 1)], s_idb[:]),
                after=flat(ln3[sub], ev_pt_free[sub], ld0) if k == 0 else ())
                for k in range(NK)]
            dst = s_ht[:].rearrange("p (k t) -> p k t", k=NK)[:, :, 128 * sub:128 * (sub + 1)]
            e = scopy(dst, pt[:], after=mms)
            ev_pt_free[sub] = e
            ev_h2t.append(e)

        _ckpt("ln3")
        bank_free = [ev_gt_free[0], ev_gt_free[1]]
        ev_h1 = []
        for fp in range(NFC // 2):
            bank = PA[fp % 2]
            pair_mms = []
            for half in range(2):
                fc = 2 * fp + half
                pair_mms += [T(lambda e, k=k, fc=fc, bank=bank, half=half:
                               e.matmul(
                    bank[:, 256 * half:256 * half + 256],
                    s_w1qt[:, DFF * k + 128 * fc:DFF * k + 128 * (fc + 1)],
                    s_ht[:, 256 * k:256 * (k + 1)], start=(k == 0),
                    stop=(k == NK - 1)),
                    after=flat(ev_h2t, ev_quant_done if m == 0 else None,
                               bank_free[fp % 2]) if (k == 0 and half == 0) else ())
                    for k in range(NK)]
            evs = []
            for half in range(2):
                fc = 2 * fp + half
                evs.append(Sc(lambda e, fc=fc, bank=bank, half=half: e.activation(
                    s_h1t[:, 256 * fc:256 * (fc + 1)],
                    bank[:, 256 * half:256 * half + 256],
                    AF.Relu, bias=s_b1c[:, fc:fc + 1], scale=1.0),
                    after=flat(pair_mms, ld0)))
                ev_h1.append(evs[-1])
            bank_free[fp % 2] = evs[-1]
        ev_ht_free = [ev_h1[-1]]
        ev_pa_free = [bank_free[0], bank_free[1], ev_pa_free[2], ev_pa_free[3]]

        _ckpt("mm1")
        accs = [(PW, 0), (PW, 512), (PA[2], 0), (PA[3], 0)]
        acc_guard = [flat(ev_nb_free[1], ev_nb_free[2]),
                     flat(ev_nb_free[1], ev_nb_free[2]),
                     flat(ev_nb_free[0]), flat(ev_pa_free[3])]
        mms_by_acc = {i: [] for i in range(4)}
        for fk in range(NFC):
            gci = m * NFC + fk
            for sub in range(2):
                for dh in range(2):
                    ai = 2 * sub + dh
                    bank, colo = accs[ai]
                    dep = flat(ev_h1[fk], w2_ev[gci],
                               acc_guard[ai] if fk == 0 else None) \
                        if (dh == 0 or fk == 0) else ()
                    mms_by_acc[ai].append(T(
                        lambda e, fk=fk, dh=dh, sub=sub, gci=gci, bank=bank,
                        colo=colo: e.matmul(
                            bank[:, colo:colo + 512],
                            s_h1t[:, 256 * fk + 128 * sub:256 * fk + 128 * (sub + 1)],
                            s_w2s[:, D * (gci % 3) + 512 * dh:
                                  D * (gci % 3) + 512 * (dh + 1)],
                            start=(fk == 0), stop=False), after=dep))
            w2_used[gci] = mms_by_acc[3][-1]
            w2_fetch(gci + 3)
        for ai in range(4):
            bank, colo = accs[ai]
            mms_by_acc[ai].append(T(lambda e, ai=ai, bank=bank, colo=colo: e.matmul(
                bank[:, colo:colo + 512], s_onesr[0:1, 0:128],
                s_b2[0:1, 512 * (ai % 2):512 * (ai % 2 + 1)],
                start=False, stop=True)))
        out_evs = []
        for sub in range(2):
            if sub == 0:
                e_ev = [scopy(s_ao[:], PW[:],
                              after=flat(mms_by_acc[0], mms_by_acc[1], ev_ao_free))]
                ev_pw_free = e_ev[0]
            else:
                e_ev = [scopy(s_ao[:, 0:512], PA[2][:],
                              after=flat(mms_by_acc[2], ev_ao_free)),
                        scopy(s_ao[:, 512:1024], PA[3][:], after=flat(mms_by_acc[3]))]
                ev_pa_free[2] = e_ev[0]
                ev_pa_free[3] = e_ev[1]
            e_ln4 = layernorm(s_ao[:], s_scr[:, 0:1024], 6 + sub, e_ev)
            xa = xs(sub)
            e_out = V(lambda e, xa=xa: e.tensor_add(
                xa, xa, s_scr[:, 0:1024]), after=[e_ln4])
            ev_h_users.append(e_out)
            ev_ao_free = [e_ln4]
            out_evs.append(dma_sync(xo_t[m, sub], xs(sub), after=[e_out]))
        ev_outdma[m] = out_evs
        ev_h_free = flat(ev_h_users)
        load_x(m + 2)
        _ckpt("macro")
    except _StopBuild:
        if _DUMP and any(b == "PTDBG" for b, _, _ in _DUMP):
            lastall = [S.last_ev.get(e) for e in ENGINES]
            scopy(s_qtok[:], PT[1][:], after=lastall)
        if _DUMP and any(b == "VCOPY" for b, _, _ in _DUMP):
            lastall = [S.last_ev.get(e) for e in ENGINES]
            scopy(s_qtok[:, 0:1024], s_vaug[:, 0:1024], after=lastall)
        if _DUMP:
            lastall = [S.last_ev.get(e) for e in ENGINES]
            dump_evs = []
            for di, (bn, c0, ncf) in enumerate(_DUMP):
                buf = _bufs[bn]
                src = buf[:].bitcast(F32)[:, c0:c0 + ncf]
                dump_evs.append(dma_sync(
                    x_out.rearrange("(r p) d -> r p d", p=128)[di][:, 0:ncf],
                    src, after=lastall))
            ev_outdma[10 ** 9] = dump_evs

    last_d = [max((ev for ev, _f, _d in S.ops[e] if ev.kind == "d"),
                  key=lambda ev: ev.idx, default=None) for e in ENGINES]
    term = flat([ev_outdma[max(ev_outdma)][-1]] if ev_outdma else
                [ev_spill_done, ev_quant_done], last_d)
    S.op("gpsimd", lambda e: e.engine_nop(), after=term, hard=term)

    S.resolve()
    with nc.Block() as block:
        @block.tensor
        def _(eng):
            S.replay("tensor", eng, sems)

        @block.vector
        def _(eng):
            S.replay("vector", eng, sems)

        @block.scalar
        def _(eng):
            S.replay("scalar", eng, sems)

        @block.gpsimd
        def _(eng):
            S.replay("gpsimd", eng, sems)

        @block.sync
        def _(eng):
            S.replay("sync", eng, sems)

    st.close()
    return nc


def _constants():
    iu = np.arange(128)
    tri = (iu[:, None] <= iu[None, :]).astype(BF)
    htri = (0.5 * (iu[:, None] <= iu[None, :])).astype(BF)
    hfull = np.full((128, 128), 0.5, BF)
    d = iu[None, :] - iu[:, None]
    bandd = ((d >= 0) & (d <= 63)).astype(BF)
    dp = iu[None, :] + 128 - iu[:, None]
    bandp = ((dp >= 0) & (dp <= 63)).astype(BF)
    bmask = np.concatenate([bandd, bandp, bandd, bandp], axis=1)
    return {
        "identb": np.eye(128, dtype=np.float32).astype(BF),
        "tri": tri, "htri": htri, "hfull": hfull, "bmask": bmask,
        "ones_c": np.ones((128, 1), BF), "ones_r": np.ones((1, 256), BF),
        "e64_r": np.eye(128, dtype=np.float32)[64:65].astype(BF).reshape(1, 128),
        "eps_r": np.full((1, 256), EPS_DEN, BF),
    }


def _spread_w(Wq):
    out = np.zeros((D, 512), np.float32)
    for h in range(H):
        g, s = h // 4, h % 4
        out[:, 128 * g + 32 * s:128 * g + 32 * s + DP] = \
            np.asarray(Wq)[DP * h:DP * (h + 1), :].T
    return out


def _spread_b(b):
    out = np.zeros((1, 512), np.float32)
    for h in range(H):
        g, s = h // 4, h % 4
        out[0, 128 * g + 32 * s:128 * g + 32 * s + DP] = \
            np.asarray(b)[DP * h:DP * (h + 1)]
    return out


def make_in_map(core, L0_attn_x, L0_attn_Wq, L0_attn_bq, L0_attn_Wk, L0_attn_bk,
                L0_attn_Wv, L0_attn_bv, L0_ffn_W1, L0_ffn_b1, L0_ffn_W2,
                L0_ffn_b2, nm=NM_FULL):
    xs = np.ascontiguousarray(
        np.asarray(L0_attn_x, dtype=np.float32)[2 * core:2 * core + 2]
        .reshape(2 * N, D)[:nm * MACRO])
    m = {
        "x_in": xs,
        "wqt": _spread_w(L0_attn_Wq),
        "wkt": _spread_w(L0_attn_Wk),
        "wvt": np.ascontiguousarray(np.asarray(L0_attn_Wv, dtype=np.float32).T),
        "w1t": np.ascontiguousarray(np.asarray(L0_ffn_W1, dtype=np.float32).T),
        "w2t": np.ascontiguousarray(np.asarray(L0_ffn_W2, dtype=np.float32).T),
        "bq_r": _spread_b(L0_attn_bq),
        "bk_r": _spread_b(L0_attn_bk),
        "bv_r": np.asarray(L0_attn_bv, dtype=np.float32).reshape(1, D),
        "b1c": np.ascontiguousarray(
            np.asarray(L0_ffn_b1, dtype=np.float32).reshape(NFC, 128).T),
        "b2_r": np.asarray(L0_ffn_b2, dtype=np.float32).reshape(1, D),
    }
    m.update(_constants())
    return m


def kernel(L0_attn_x, L0_attn_Wq, L0_attn_bq, L0_attn_Wk, L0_attn_bk,
           L0_attn_Wv, L0_attn_bv, L0_ffn_W1, L0_ffn_b1, L0_ffn_W2, L0_ffn_b2):
    nc = build_nc(NM_FULL)
    in_maps = [make_in_map(c, L0_attn_x, L0_attn_Wq, L0_attn_bq, L0_attn_Wk,
                           L0_attn_bk, L0_attn_Wv, L0_attn_bv, L0_ffn_W1,
                           L0_ffn_b1, L0_ffn_W2, L0_ffn_b2)
               for c in range(NCORES)]
    res = run_bass_kernel_spmd(nc, in_maps, core_ids=list(range(NCORES)))
    out = np.concatenate([np.asarray(res.results[c]["x_out"]).reshape(2, N, D)
                          for c in range(NCORES)], axis=0)
    return out.astype(np.float32)
